# version 10 (final)
# baseline (speedup 1.0000x reference)
import numpy as np
import contextlib
import concourse.bass as bass
import concourse.mybir as mybir
from concourse.bass_utils import run_bass_kernel_spmd

F32 = mybir.dt.float32
BF16 = mybir.dt.bfloat16
AF = mybir.ActivationFunctionType
ALU = mybir.AluOpType
AX = mybir.AxisListType.X

S, D = 4096, 2048
NT = S // 128
ND = D // 128
NEG = -1.0e30


class Ch:

    def __init__(self, nc):
        self.nc = nc
        self.ops = {k: [] for k in ("d", "g", "t", "v", "s")}
        self.cnt = {k: 0 for k in ("d", "g", "t", "v", "s")}
        self.sem = {}
        self.last = None
        self.eng = {
            "d": nc.sync,
            "g": nc.gpsimd,
            "t": nc.tensor,
            "v": nc.vector,
            "s": nc.scalar,
        }

    def _emit(self, key, fn, inc):
        last = self.last
        sem = self.sem[key]
        eng = self.eng[key]
        need_wait = last is not None and not (
            last[0] == key and key in ("t", "d", "g")
        )
        if need_wait:
            lk, lv = last
            lsem = self.sem[lk]
            self.ops[key].append(lambda e=eng, s=lsem, v=lv: e.wait_ge(s, v))
        self.cnt[key] += inc
        cval = self.cnt[key]

        def go(fn=fn, sem=sem, inc=inc):
            fn().then_inc(sem, inc)

        self.ops[key].append(go)
        self.last = (key, cval)

    def ld(self, out, in_):
        self._emit("d", lambda o=out, i=in_: self.nc.sync.dma_start(out=o, in_=i), 16)

    def st(self, out, in_):
        self._emit("g", lambda o=out, i=in_: self.nc.gpsimd.dma_start(out=o, in_=i), 16)

    def mm(self, out, lhsT, rhs, start, stop):
        self._emit(
            "t",
            lambda o=out, l=lhsT, r=rhs, a=start, b=stop: self.nc.tensor.matmul(
                o, l, r, start=a, stop=b
            ),
            1,
        )

    def tr(self, out, in_, ident):
        self._emit(
            "t", lambda o=out, i=in_, d=ident: self.nc.tensor.transpose(o, i, d), 1
        )

    def tt(self, out, a, b, op):
        self._emit(
            "v",
            lambda o=out, x=a, y=b, p=op: self.nc.vector.tensor_tensor(o, x, y, p),
            1,
        )

    def ts(self, out, in0, scalar, op):
        self._emit(
            "v",
            lambda o=out, i=in0, s=scalar, p=op: self.nc.vector.tensor_scalar(
                o, i, s, None, p
            ),
            1,
        )

    def stt(self, out, in0, scalar, in1, op0, op1):
        self._emit(
            "v",
            lambda o=out, i=in0, s=scalar, j=in1, p=op0, q=op1: self.nc.vector.scalar_tensor_tensor(
                o, i, s, j, p, q
            ),
            1,
        )

    def red(self, out, in_, op):
        self._emit(
            "v",
            lambda o=out, i=in_, p=op: self.nc.vector.tensor_reduce(o, i, AX, p),
            1,
        )

    def cp(self, out, in_):
        self._emit("v", lambda o=out, i=in_: self.nc.vector.tensor_copy(o, i), 1)

    def recip(self, out, in_):
        self._emit("v", lambda o=out, i=in_: self.nc.vector.reciprocal(o, i), 1)

    def mset(self, ap, c):
        self._emit("v", lambda a=ap, k=c: self.nc.vector.memset(a, k), 1)

    def act(self, out, in_, func, bias=None, scale=None):
        def go(o=out, i=in_, f=func, b=bias, s=scale):
            kw = {}
            if b is not None:
                kw["bias"] = b
            if s is not None:
                kw["scale"] = s
            return self.nc.scalar.activation(o, i, f, **kw)

        self._emit("s", go, 1)

    def smul(self, out, in_, c):
        self._emit("s", lambda o=out, i=in_, k=c: self.nc.scalar.mul(o, i, k), 1)

    def sadd(self, out, in_, c):
        self._emit("s", lambda o=out, i=in_, k=c: self.nc.scalar.add(o, i, k), 1)


def build(nc):
    dt = lambda n, sh, ty, k: nc.dram_tensor(n, sh, ty, kind=k).ap()
    x_in = dt("x", [S, D], F32, "ExternalInput")
    y_out = dt("y", [S, D], F32, "ExternalOutput")
    qkvT = dt("qkvT", [D, 3 * D], F32, "ExternalInput")
    qkv_b = dt("qkv_b", [1, 3 * D], F32, "ExternalInput")
    outT = dt("outT", [D, D], F32, "ExternalInput")
    out_b = dt("out_b", [1, D], F32, "ExternalInput")
    slopes = dt("slopes", [1, 16], F32, "ExternalInput")
    qwT = dt("qwT", [D, D], F32, "ExternalInput")
    kwT = dt("kwT", [D, 512], F32, "ExternalInput")
    vwT = dt("vwT", [D, 512], F32, "ExternalInput")
    owT = dt("owT", [D, D], F32, "ExternalInput")
    gwT = dt("gwT", [D, 8], F32, "ExternalInput")
    wgT = dt("wgT", [8, D, 1408], F32, "ExternalInput")
    wuT = dt("wuT", [8, D, 1408], F32, "ExternalInput")
    wdT = dt("wdT", [8, 1408, D], F32, "ExternalInput")
    anw = dt("anw", [128, 16], F32, "ExternalInput")
    fnw = dt("fnw", [128, 16], F32, "ExternalInput")
    qnw_d = dt("qnw", [128, 1], F32, "ExternalInput")
    knw_d = dt("knw", [128, 1], F32, "ExternalInput")
    ident_d = dt("ident", [128, 128], F32, "ExternalInput")
    posr_d = dt("posr", [1, S], F32, "ExternalInput")
    cosT_d = dt("cosT", [128, S], F32, "ExternalInput")
    sinT_d = dt("sinT", [128, S], F32, "ExternalInput")
    mask_d = dt("maskc", [128, 4 * 512], F32, "ExternalInput")
    xsA = dt("xsA", [S, D], F32, "Internal")
    xsB = dt("xsB", [S, D], F32, "Internal")
    hid = dt("hid", [S, D], F32, "Internal")
    hidB = dt("hidB", [S, D], F32, "Internal")
    hTs = dt("hTs", [D, S], BF16, "Internal")
    qTs = dt("qTs", [D, S], BF16, "Internal")
    kTs = dt("kTs", [D, S], BF16, "Internal")
    Vs = dt("Vs", [S, D], BF16, "Internal")
    AOT = dt("AOT", [D, S], BF16, "Internal")
    moes = dt("moes", [S, D], F32, "Internal")

    hT3 = hTs.rearrange("(c p) t -> p c t", p=128)
    uid = [0]

    def sb(shape, ty):
        uid[0] += 1
        return nc.sbuf_tensor(f"b{uid[0]}", shape, ty)

    es = contextlib.ExitStack()
    with es:
        idb = es.enter_context(sb([128, 128], F32))
        xt = es.enter_context(sb([128, D], F32))
        xc = es.enter_context(sb([128, D], F32))
        sq = es.enter_context(sb([128, D], F32))
        mn = es.enter_context(sb([128, 1], F32))
        vv = es.enter_context(sb([128, 1], F32))
        rs = es.enter_context(sb([128, 1], F32))
        tbuf = es.enter_context(sb([128, D], BF16))
        nwa = es.enter_context(sb([128, 16], F32))
        nwf = es.enter_context(sb([128, 16], F32))
        qnw = es.enter_context(sb([128, 1], F32))
        knw = es.enter_context(sb([128, 1], F32))
        maskb = es.enter_context(sb([128, 2048], F32))
        ALC = es.enter_context(sb([128, 512], F32))
        onesr = es.enter_context(sb([1, 128], BF16))
        onesc = es.enter_context(sb([128, 1], BF16))
        wf = es.enter_context(sb([128, 512], F32))
        qnb = es.enter_context(sb([128, 128], F32))
        sqh = es.enter_context(sb([128, 128], F32))
        qtb = es.enter_context(sb([128, 128], F32))
        t64a = es.enter_context(sb([64, 128], F32))
        t64b = es.enter_context(sb([64, 128], F32))
        qout = es.enter_context(sb([128, 128], BF16))
        expb = es.enter_context(sb([128, 512], BF16))
        recb = es.enter_context(sb([1, 512], BF16))
        aob = es.enter_context(sb([128, 512], BF16))
        hb512 = es.enter_context(sb([128, 512], F32))
        recf = es.enter_context(sb([1, 512], F32))
        cvb = es.enter_context(sb([128, 16], F32))
        qrot = es.enter_context(sb([128, 128], F32))
        posf = es.enter_context(sb([1, S], F32))
        slof = es.enter_context(sb([1, 16], F32))
        onesfr = es.enter_context(sb([1, 128], F32))
        nal = es.enter_context(sb([1, 512], F32))
        gwb = es.enter_context(sb([128, 128], BF16))
        Wbuf = es.enter_context(sb([128, NT * 8], F32))
        g8a = es.enter_context(sb([128, 8], F32))
        g8b = es.enter_context(sb([128, 8], F32))
        g8c = es.enter_context(sb([128, 8], F32))
        m1 = es.enter_context(sb([128, 1], F32))
        m2 = es.enter_context(sb([128, 1], F32))
        wsum = es.enter_context(sb([128, 1], F32))
        ps0 = es.enter_context(nc.psum_tensor("p0", [128, 512], F32))
        ps1 = es.enter_context(nc.psum_tensor("p1", [128, 512], F32))
        ps2 = es.enter_context(nc.psum_tensor("p2", [128, 512], F32))
        ps3 = es.enter_context(nc.psum_tensor("p3", [128, 512], F32))
        psT = es.enter_context(nc.psum_tensor("pT", [128, 128], F32))
        sems = {}
        for k in ("d", "g", "t", "v", "s"):
            sems[k] = es.enter_context(nc.semaphore(f"sem_{k}"))

        ch = Ch(nc)
        ch.sem = sems

        ch.ld(idb[:], ident_d)
        ch.ld(nwa[:], anw)
        ch.ld(nwf[:], fnw)
        ch.ld(qnw[:], qnw_d)
        ch.ld(knw[:], knw_d)
        ch.ld(maskb[:], mask_d)
        ch.mset(onesr[:], 1.0)
        ch.mset(onesc[:], 1.0)
        _cvals = [1e-5, 1e-6, 1.0 / 2048, -0.5, 1.0 / 128, float(1.0 / np.sqrt(128.0)),
                  -1.0, 0.5, 1e-9, 1.0, 0.0]
        for _i, _v in enumerate(_cvals):
            ch.mset(cvb[:, _i:_i + 1], _v)
            nc.const_aps.register(F32, _v, cvb[:, _i:_i + 1])
        ch.mset(onesfr[:], 1.0)
        ch.ld(posf[:], posr_d)
        ch.ld(slof[:], slopes)
        for kb in range(32):
            ch.mm(psT[:, 0:16], posf[0:1, kb * 128:(kb + 1) * 128], slof[:], True, True)
            ch.cp(ALC[:, kb * 16:(kb + 1) * 16], psT[:, 0:16])

        def norm_to_hT(src, do_ln, do_rms, nw, hid_dst):
            for tt in range(NT):
                r0 = tt * 128
                ch.ld(xt[:], src[r0:r0 + 128, :])
                cur = xt
                if do_ln:
                    ch.red(mn[:], xt[:], ALU.add)
                    ch.smul(mn[:], mn[:], 1.0 / D)
                    ch.ts(xc[:], xt[:], mn[:], ALU.subtract)
                    ch.act(sq[:], xc[:], AF.Square)
                    ch.red(vv[:], sq[:], ALU.add)
                    ch.act(rs[:], vv[:], AF.Ln, bias=1e-5, scale=1.0 / D)
                    ch.act(rs[:], rs[:], AF.Exp, scale=-0.5)
                    ch.ts(xc[:], xc[:], rs[:], ALU.mult)
                    cur = xc
                    if hid_dst is not None:
                        ch.st(hid_dst[r0:r0 + 128, :], xc[:])
                if do_rms:
                    ch.act(sq[:], cur[:], AF.Square)
                    ch.red(vv[:], sq[:], ALU.add)
                    ch.act(rs[:], vv[:], AF.Ln, bias=1e-6, scale=1.0 / D)
                    ch.act(rs[:], rs[:], AF.Exp, scale=-0.5)
                    ch.ts(xc[:], cur[:], rs[:], ALU.mult)
                    cur = xc
                for c in range(ND):
                    ch.tr(psT[:], cur[:, c * 128:(c + 1) * 128], idb[:])
                    if nw is not None:
                        ch.ts(tbuf[:, c * 128:(c + 1) * 128], psT[:], nw[:, c:c + 1], ALU.mult)
                    else:
                        ch.cp(tbuf[:, c * 128:(c + 1) * 128], psT[:])
                ch.st(
                    hT3[:, :, tt * 128:(tt + 1) * 128],
                    tbuf[:].rearrange("p (c f) -> p c f", f=128),
                )

        def rope_head(norm_col, tt, cosb, sinb):
            ch.ts(qtb[:], qtb[:], norm_col, ALU.mult)
            cs = cosb[:, tt * 128:(tt + 1) * 128]
            sn = sinb[:, tt * 128:(tt + 1) * 128]
            ch.st(qrot[0:64, :], qtb[64:128, :])
            ch.st(qrot[64:128, :], qtb[0:64, :])
            ch.tt(sqh[:], qtb[:], cs, ALU.mult)
            ch.tt(qnb[:], qrot[:], sn, ALU.mult)
            ch.tt(qout[:], sqh[:], qnb[:], ALU.add)

        def proj_phase(wsrc_list, with_bias, post, wqb):
            for fc, (wsrc, c0) in enumerate(wsrc_list):
                for d in range(ND):
                    ch.ld(wf[:], wsrc[d * 128:(d + 1) * 128, c0:c0 + 512])
                    ch.cp(wqb[:, d * 512:(d + 1) * 512], wf[:])
                for tt in range(NT):
                    ch.ld(
                        tbuf[:].rearrange("p (c f) -> p c f", f=128),
                        hT3[:, :, tt * 128:(tt + 1) * 128],
                    )
                    for d in range(ND):
                        ch.mm(
                            ps0[:],
                            tbuf[:, d * 128:(d + 1) * 128],
                            wqb[:, d * 512:(d + 1) * 512],
                            d == 0,
                            (d == ND - 1) and (with_bias is None),
                        )
                    if with_bias is not None:
                        ch.mm(ps0[:], onesr[:], with_bias[:, fc * 512:(fc + 1) * 512], False, True)
                    post(fc, tt)

        def attn_pass(nheads, kv_of, alibi):
            with sb([128, S], BF16) as kTg, sb([128, S], BF16) as qTh, sb([128, S], BF16) as vgb:
                seen = {}
                for h in range(nheads):
                    g = kv_of(h)
                    if seen.get(g) is None:
                        seen[g] = 1
                        ch.ld(kTg[:], kTs[g * 128:(g + 1) * 128, :])
                        ch.ld(
                            vgb[:].rearrange("p (n f) -> p n f", f=128),
                            Vs.rearrange("(n p) f -> p n f", p=128)[:, :, g * 128:(g + 1) * 128],
                        )
                    ch.ld(qTh[:], qTs[h * 128:(h + 1) * 128, :])
                    for qc in range(8):
                        nb = 4 * (qc + 1)
                        if alibi:
                            ch.mm(ps3[0:1, :], slof[0:1, h:h + 1], posf[0:1, qc * 512:(qc + 1) * 512], True, True)
                            ch.smul(nal[0:1, :], ps3[0:1, :], -1.0)
                        for kb in range(nb):
                            ch.mm(
                                ps1[:],
                                kTg[:, kb * 128:(kb + 1) * 128],
                                qTh[:, qc * 512:(qc + 1) * 512],
                                True,
                                not alibi,
                            )
                            if alibi:
                                ch.mm(ps1[:], onesfr[:], nal[0:1, :], False, True)
                            j = kb - 4 * qc
                            if j >= 0:
                                ch.tt(ps1[:], ps1[:], maskb[:, j * 512:(j + 1) * 512], ALU.add)
                            if alibi:
                                ch.act(
                                    expb[:], ps1[:], AF.Exp,
                                    bias=ALC[:, kb * 16 + h:kb * 16 + h + 1],
                                )
                            else:
                                ch.act(expb[:], ps1[:], AF.Exp, scale=float(1.0 / np.sqrt(128.0)))
                            ch.mm(ps3[0:1, :], onesc[:], expb[:], kb == 0, kb == nb - 1)
                            ch.mm(ps2[:], vgb[:, kb * 128:(kb + 1) * 128], expb[:], kb == 0, kb == nb - 1)
                        ch.recip(recf[:], ps3[0:1, :])
                        ch.cp(recb[:], recf[:])
                        ch.mm(ps0[:], onesr[:], recb[:], True, True)
                        ch.cp(hb512[:], ps0[:])
                        ch.tt(aob[:], ps2[:], hb512[:], ALU.mult)
                        ch.st(AOT[h * 128:(h + 1) * 128, qc * 512:(qc + 1) * 512], aob[:])

        def oproj_phase(wT, bias_b, res_src, dst):
            AOT3 = AOT.rearrange("(c p) t -> p c t", p=128)
            with sb([128, 16 * 512], BF16) as wqb:
                for dc in range(4):
                    for d in range(ND):
                        ch.ld(wf[:], wT[d * 128:(d + 1) * 128, dc * 512:(dc + 1) * 512])
                        ch.cp(wqb[:, d * 512:(d + 1) * 512], wf[:])
                    for tt in range(NT):
                        r0 = tt * 128
                        ch.ld(
                            tbuf[:].rearrange("p (c f) -> p c f", f=128),
                            AOT3[:, :, r0:r0 + 128],
                        )
                        for d in range(ND):
                            ch.mm(
                                ps0[:],
                                tbuf[:, d * 128:(d + 1) * 128],
                                wqb[:, d * 512:(d + 1) * 512],
                                d == 0,
                                (d == ND - 1) and (bias_b is None),
                            )
                        if bias_b is not None:
                            ch.mm(ps0[:], onesr[:], bias_b[:, dc * 512:(dc + 1) * 512], False, True)
                        ch.ld(hb512[:], res_src[r0:r0 + 128, dc * 512:(dc + 1) * 512])
                        ch.tt(hb512[:], hb512[:], ps0[:], ALU.add)
                        ch.st(dst[r0:r0 + 128, dc * 512:(dc + 1) * 512], hb512[:])

        def ffn_block(xin, xout):
            norm_to_hT(xin, True, True, nwa, hid)

            with sb([128, 16 * 512], BF16) as wqb, sb([128, S], F32) as cosb, sb([128, S], F32) as sinb:
                ch.ld(cosb[:], cosT_d)
                ch.ld(sinb[:], sinT_d)

                def post(fc, tt):
                    if fc < 4:
                        for hh in range(4):
                            h = fc * 4 + hh
                            phs = ps0[:, hh * 128:(hh + 1) * 128]
                            ch.act(sqh[:], phs, AF.Square)
                            ch.red(vv[:], sqh[:], ALU.add)
                            ch.act(rs[:], vv[:], AF.Ln, bias=1e-6, scale=1.0 / 128)
                            ch.act(rs[:], rs[:], AF.Exp, scale=-0.5)
                            ch.ts(qnb[:], phs, rs[:], ALU.mult)
                            ch.tr(psT[:], qnb[:], idb[:])
                            ch.cp(qtb[:], psT[:])
                            rope_head(qnw[:], tt, cosb, sinb)
                            ch.st(qTs[h * 128:(h + 1) * 128, tt * 128:(tt + 1) * 128], qout[:])
                    elif fc == 4:
                        for hh in range(4):
                            phs = ps0[:, hh * 128:(hh + 1) * 128]
                            ch.act(sqh[:], phs, AF.Square)
                            ch.red(vv[:], sqh[:], ALU.add)
                            ch.act(rs[:], vv[:], AF.Ln, bias=1e-6, scale=1.0 / 128)
                            ch.act(rs[:], rs[:], AF.Exp, scale=-0.5)
                            ch.ts(qnb[:], phs, rs[:], ALU.mult)
                            ch.tr(psT[:], qnb[:], idb[:])
                            ch.cp(qtb[:], psT[:])
                            rope_head(knw[:], tt, cosb, sinb)
                            ch.st(kTs[hh * 128:(hh + 1) * 128, tt * 128:(tt + 1) * 128], qout[:])
                    else:
                        ch.cp(aob[:], ps0[:])
                        ch.st(Vs[tt * 128:(tt + 1) * 128, 0:512], aob[:])

                proj_phase(
                    [(qwT, 0), (qwT, 512), (qwT, 1024), (qwT, 1536), (kwT, 0), (vwT, 0)],
                    None,
                    post,
                    wqb,
                )
            attn_pass(16, lambda h: h // 4, False)
            oproj_phase(owT, None, hid, hidB)
            norm_to_hT(hidB, False, True, nwf, None)

            ch.ld(wf[:, 0:128].rearrange("p (c e) -> p c e", e=8), gwT.rearrange("(c p) e -> p c e", p=128))
            ch.cp(gwb[:], wf[:, 0:128])
            for tt in range(NT):
                ch.ld(
                    tbuf[:].rearrange("p (c f) -> p c f", f=128),
                    hT3[:, :, tt * 128:(tt + 1) * 128],
                )
                for d in range(ND):
                    ch.mm(ps0[:, 0:8], tbuf[:, d * 128:(d + 1) * 128], gwb[:, d * 8:(d + 1) * 8], d == 0, d == ND - 1)
                p8 = ps0[:, 0:8]
                ch.red(m1[:], p8, ALU.max)
                ch.smul(m2[:], m1[:], -1.0)
                ch.act(g8a[:], p8, AF.Exp, bias=m2[:])
                ch.red(wsum[:], g8a[:], ALU.add)
                ch.recip(rs[:], wsum[:])
                ch.ts(g8a[:], g8a[:], rs[:], ALU.mult)
                ch.red(m1[:], g8a[:], ALU.max)
                ch.ts(g8b[:], g8a[:], m1[:], ALU.is_equal)
                ch.tt(g8c[:], g8a[:], g8b[:], ALU.subtract)
                ch.red(m2[:], g8c[:], ALU.max)
                ch.tt(wsum[:], m1[:], m2[:], ALU.add)
                ch.sadd(wsum[:], wsum[:], 1e-9)
                ch.recip(rs[:], wsum[:])
                ch.ts(g8b[:], g8a[:], m2[:], ALU.is_ge)
                ch.tt(g8b[:], g8a[:], g8b[:], ALU.mult)
                ch.ts(Wbuf[:, tt * 8:(tt + 1) * 8], g8b[:], rs[:], ALU.mult)

            with sb([128, 176 * 128], BF16) as wgb, sb([128, 176 * 128], BF16) as wub, \
                 sb([128, 11 * 512], BF16) as wdb, sb([128, 16 * 512], BF16) as hTch, \
                 sb([128, 11 * 512], BF16) as gtb, sb([128, 512], F32) as egf, \
                 sb([128, 512], F32) as euf, sb([128, 512], F32) as mo_sb:
                for e in range(8):
                    for it in range(11):
                        for d in range(ND):
                            ch.ld(wf[:, 0:128], wgT[e, d * 128:(d + 1) * 128, it * 128:(it + 1) * 128])
                            ch.cp(wgb[:, (it * 16 + d) * 128:(it * 16 + d + 1) * 128], wf[:, 0:128])
                            ch.ld(wf[:, 0:128], wuT[e, d * 128:(d + 1) * 128, it * 128:(it + 1) * 128])
                            ch.cp(wub[:, (it * 16 + d) * 128:(it * 16 + d + 1) * 128], wf[:, 0:128])
                    for chk in range(8):
                        c0 = chk * 512
                        ch.ld(
                            hTch[:].rearrange("p (c f) -> p c f", f=512),
                            hT3[:, :, c0:c0 + 512],
                        )
                        for it in range(11):
                            for d in range(ND):
                                ch.mm(ps0[:], wgb[:, (it * 16 + d) * 128:(it * 16 + d + 1) * 128], hTch[:, d * 512:(d + 1) * 512], d == 0, d == ND - 1)
                            ch.act(egf[:], ps0[:], AF.Silu)
                            for d in range(ND):
                                ch.mm(ps1[:], wub[:, (it * 16 + d) * 128:(it * 16 + d + 1) * 128], hTch[:, d * 512:(d + 1) * 512], d == 0, d == ND - 1)
                            ch.cp(euf[:], ps1[:])
                            ch.tt(gtb[:, it * 512:(it + 1) * 512], egf[:], euf[:], ALU.mult)
                        for dq in range(4):
                            for it in range(11):
                                ch.ld(wf[:], wdT[e, it * 128:(it + 1) * 128, dq * 512:(dq + 1) * 512])
                                ch.cp(wdb[:, it * 512:(it + 1) * 512], wf[:])
                            for tl in range(4):
                                tglob = chk * 4 + tl
                                wcol = Wbuf[:, tglob * 8 + e:tglob * 8 + e + 1]
                                for it in range(11):
                                    ch.mm(ps2[:], gtb[:, it * 512 + tl * 128:it * 512 + tl * 128 + 128], wdb[:, it * 512:(it + 1) * 512], it == 0, it == 10)
                                msl = moes[tglob * 128:(tglob + 1) * 128, dq * 512:(dq + 1) * 512]
                                if e == 0:
                                    ch.ts(mo_sb[:], ps2[:], wcol, ALU.mult)
                                else:
                                    ch.ld(mo_sb[:], msl)
                                    ch.stt(mo_sb[:], ps2[:], wcol, mo_sb[:], ALU.mult, ALU.add)
                                ch.st(msl, mo_sb[:])
            for tt in range(NT):
                r0 = tt * 128
                ch.ld(xt[:], hidB[r0:r0 + 128, :])
                ch.ld(xc[:], moes[r0:r0 + 128, :])
                ch.tt(xt[:], xt[:], xc[:], ALU.add)
                ch.smul(xt[:], xt[:], 0.5)
                ch.ld(xc[:], xin[r0:r0 + 128, :])
                ch.tt(xt[:], xt[:], xc[:], ALU.add)
                ch.st(xout[r0:r0 + 128, :], xt[:])

        def attn_block(xin, xout):
            norm_to_hT(xin, True, False, None, None)
            with sb([128, 16 * 512], BF16) as wqb, sb([1, 2048], F32) as stageA, \
                 sb([1, 3 * D], BF16) as qkvb_b, sb([1, D], BF16) as outb_b:
                for seg in range(3):
                    ch.ld(stageA[:], qkv_b[0:1, seg * 2048:(seg + 1) * 2048])
                    ch.cp(qkvb_b[:, seg * 2048:(seg + 1) * 2048], stageA[:])
                ch.ld(stageA[:], out_b)
                ch.cp(outb_b[:], stageA[:])

                def post(fc, tt):
                    if fc < 4:
                        for hh in range(4):
                            h = fc * 4 + hh
                            ch.smul(qnb[:], ps0[:, hh * 128:(hh + 1) * 128], float(1.0 / np.sqrt(128.0)))
                            ch.tr(psT[:], qnb[:], idb[:])
                            ch.cp(qout[:], psT[:])
                            ch.st(qTs[h * 128:(h + 1) * 128, tt * 128:(tt + 1) * 128], qout[:])
                    elif fc < 8:
                        for hh in range(4):
                            h = (fc - 4) * 4 + hh
                            ch.cp(qnb[:], ps0[:, hh * 128:(hh + 1) * 128])
                            ch.tr(psT[:], qnb[:], idb[:])
                            ch.cp(qout[:], psT[:])
                            ch.st(kTs[h * 128:(h + 1) * 128, tt * 128:(tt + 1) * 128], qout[:])
                    else:
                        ch.cp(aob[:], ps0[:])
                        ch.st(Vs[tt * 128:(tt + 1) * 128, (fc - 8) * 512:(fc - 7) * 512], aob[:])

                proj_phase([(qkvT, i * 512) for i in range(12)], qkvb_b, post, wqb)
                attn_pass(16, lambda h: h, True)
                oproj_phase(outT, outb_b, xin, xout)

        ffn_block(x_in, xsA)
        attn_block(xsA, xsB)
        ffn_block(xsB, y_out)

        with nc.Block() as block:

            @block.sync
            def _(sync):
                for f in ch.ops["d"]:
                    f()

            @block.gpsimd
            def _(gpsimd):
                for f in ch.ops["g"]:
                    f()
                gpsimd.wait_ge(sems["g"], ch.cnt["g"])

            @block.tensor
            def _(tensor):
                for f in ch.ops["t"]:
                    f()

            @block.vector
            def _(vector):
                for f in ch.ops["v"]:
                    f()

            @block.scalar
            def _(scalar):
                for f in ch.ops["s"]:
                    f()

    return nc


def kernel(**inp):
    x = np.ascontiguousarray(inp["L0_attn_hidden_states"])
    T = lambda a: np.ascontiguousarray(np.asarray(a).T)
    base = {
        "qkvT": T(inp["L0_attn_qkv_w"]),
        "qkv_b": np.ascontiguousarray(inp["L0_attn_qkv_b"]).reshape(1, -1),
        "outT": T(inp["L0_attn_out_w"]),
        "out_b": np.ascontiguousarray(inp["L0_attn_out_b"]).reshape(1, -1),
        "slopes": np.ascontiguousarray(inp["L0_attn_alibi_slopes"]).reshape(1, -1),
        "qwT": T(inp["L0_ffn_q_w"]),
        "kwT": T(inp["L0_ffn_k_w"]),
        "vwT": T(inp["L0_ffn_v_w"]),
        "owT": T(inp["L0_ffn_o_w"]),
        "gwT": T(inp["L0_ffn_gate_w"]),
        "wgT": np.ascontiguousarray(np.asarray(inp["L0_ffn_wg"]).transpose(0, 2, 1)),
        "wuT": np.ascontiguousarray(np.asarray(inp["L0_ffn_wu"]).transpose(0, 2, 1)),
        "wdT": np.ascontiguousarray(np.asarray(inp["L0_ffn_wd"]).transpose(0, 2, 1)),
        "anw": np.ascontiguousarray(
            np.asarray(inp["L0_ffn_attn_norm_w"]).reshape(16, 128).T
        ),
        "fnw": np.ascontiguousarray(
            np.asarray(inp["L0_ffn_ffn_norm_w"]).reshape(16, 128).T
        ),
        "qnw": np.ascontiguousarray(inp["L0_ffn_q_norm_w"]).reshape(128, 1),
        "knw": np.ascontiguousarray(inp["L0_ffn_k_norm_w"]).reshape(128, 1),
        "ident": np.eye(128, dtype=np.float32),
        "posr": np.arange(S, dtype=np.float32).reshape(1, S),
    }
    freqs = (1.0 / 1000000.0 ** (np.arange(0, 128, 2, dtype=np.float32) / 128.0)).astype(
        np.float32
    )
    f = np.outer(np.arange(S, dtype=np.float32), freqs)
    cosT = np.cos(f).T.astype(np.float32)
    sinT = np.sin(f).T.astype(np.float32)
    base["cosT"] = np.ascontiguousarray(np.vstack([cosT, cosT]))
    base["sinT"] = np.ascontiguousarray(np.vstack([-sinT, sinT]))
    masks = np.zeros((128, 4 * 512), dtype=np.float32)
    for j in range(4):
        kp = np.arange(128)[:, None]
        qf = np.arange(512)[None, :]
        masks[:, j * 512:(j + 1) * 512] = np.where(j * 128 + kp <= qf, 0.0, NEG)
    base["maskc"] = masks

    nc = bass.Bass("TRN2", target_bir_lowering=False)
    build(nc)
    in_maps = []
    for c in range(8):
        m = dict(base)
        m["x"] = np.ascontiguousarray(x[c % 4])
        in_maps.append(m)
    res = run_bass_kernel_spmd(nc, in_maps, core_ids=list(range(8)))
    out = np.stack([res.results[b]["y"] for b in range(4)], axis=0)
    return out.astype(np.float32)
